# version 17
# baseline (speedup 1.0000x reference)
import contextlib
import numpy as np
import ml_dtypes
import concourse.bass as bass
import concourse.mybir as mybir
from concourse.bass_utils import run_bass_kernel_spmd

F32 = mybir.dt.float32
BF16 = mybir.dt.bfloat16
AF = mybir.ActivationFunctionType
AX = mybir.AxisListType.X
OP = mybir.AluOpType

DEBUG = False

B, S, E = 4, 4096, 2048
H, KV, D = 16, 4, 128
THETA = 1e6
NEXP, INTER = 8, 1024
TC = 2048
LH = 512
LTOT = TC + LH
SCALE = 1.0 / float(np.sqrt(np.float32(D)))
NEG = -1e9


class Buf:
    __slots__ = ("w", "r", "name")

    def __init__(self, name=""):
        self.w = None
        self.r = []
        self.name = name


class Prog:
    CENG = ("tensor", "vector", "scalar", "gpsimd")
    NDG = 6
    NDS = 4

    def __init__(self, nc):
        self.nc = nc
        self.q = {e: [] for e in ("tensor", "vector", "scalar", "gpsimd", "sync")}
        self.sem = {}
        self.cnt = {}
        self.waited = {e: {} for e in self.q}
        names = ["Ep", "Ev", "Ea", "Eg", "Es"]
        names += [f"Dg{i}" for i in range(self.NDG)]
        names += [f"Ds{i}" for i in range(self.NDS)]
        for name in names:
            self.sem[name] = nc.alloc_semaphore(name)
            self.cnt[name] = 0
        self.esem = {"tensor": "Ep", "vector": "Ev", "scalar": "Ea",
                     "gpsimd": "Eg", "sync": "Es"}
        self.rr = {"Dg": 0, "Ds": 0}

    def _emit_wait(self, eng, pair):
        sname, val = pair
        if val <= 0 or self.waited[eng].get(sname, 0) >= val:
            return
        self.waited[eng][sname] = val
        h = self.sem[sname]
        self.q[eng].append(lambda E, h=h, v=val: E.wait_ge(h, v))

    def step(self, eng, fn, reads=(), writes=(), waits=(), dma=None, ndma=1):
        need = []
        for b in reads:
            if b.w is not None:
                need.extend(b.w)
        for b in writes:
            if b.w is not None:
                need.extend(b.w)
            for t in b.r:
                need.extend(t)
        for t in waits:
            need.extend(t)
        if dma is not None:
            assign = []
            for _ in range(ndma):
                i = self.rr[dma]
                self.rr[dma] = (i + 1) % (self.NDG if dma == "Dg" else self.NDS)
                sname = f"{dma}{i}"
                need.append((sname, self.cnt[sname]))
                assign.append(sname)
        best = {}
        for sname, val in need:
            if best.get(sname, -1) < val:
                best[sname] = val
        for sname in sorted(best):
            self._emit_wait(eng, (sname, best[sname]))
        mysem = self.esem[eng]
        if dma is not None:
            pairs = []
            for sname in assign:
                self.cnt[sname] += 16
                pairs.append((sname, self.cnt[sname]))
            tok = tuple(pairs)
            hs = [self.sem[s] for s in assign]

            def run(E, fn=fn, hs=hs):
                r = fn(E)
                if not isinstance(r, list):
                    r = [r]
                for ins, h in zip(r, hs, strict=True):
                    ins.then_inc(h, 16)
        else:
            self.cnt[mysem] += 1
            tok = ((mysem, self.cnt[mysem]),)
            h = self.sem[mysem]

            def run(E, fn=fn, h=h):
                fn(E).then_inc(h, 1)
        self.q[eng].append(run)
        for b in reads:
            b.r.append(tok)
        for b in writes:
            b.w = tok
            b.r = []
        return tok

    def barrier(self):
        pairs = [(s, self.cnt[s]) for s in self.cnt]
        for e in self.q:
            for t in pairs:
                self._emit_wait(e, t)

    def finalize(self, block):
        for ename, meth in (("tensor", block.tensor), ("vector", block.vector),
                            ("scalar", block.scalar), ("gpsimd", block.gpsimd),
                            ("sync", block.sync)):
            ops = self.q[ename]

            def body(E, ops=ops):
                for f in ops:
                    f(E)
            meth(body)


def build_nc():
    nc = bass.Bass("TRN2", target_bir_lowering=False, debug=False)
    dt = nc.dram_tensor
    x_in = dt("x", [LTOT, E], F32, kind="ExternalInput").ap()
    wqkv = dt("wqkv", [12, 128, 16, 256], F32, kind="ExternalInput").ap()
    wo_in = dt("wo", [4, 128, 16, 512], F32, kind="ExternalInput").ap()
    wrt = dt("wrt", [128, 16, NEXP], F32, kind="ExternalInput").ap()
    wgu_in = dt("wgu", [NEXP, 2, 128, 4, 4, 1024], F32, kind="ExternalInput").ap()
    wd_in = dt("wd", [NEXP, 128, 8, 2048], F32, kind="ExternalInput").ap()
    cost = dt("cost", [128, LTOT], F32, kind="ExternalInput").ap()
    sint = dt("sint", [128, LTOT], F32, kind="ExternalInput").ap()
    ident_in = dt("ident", [128, 128], BF16, kind="ExternalInput").ap()
    maskw_in = dt("maskw", [128, 128], F32, kind="ExternalInput").ap()
    maskc_in = dt("maskc", [128, 128], F32, kind="ExternalInput").ap()
    hbias_in = dt("hbias", [128, 128], F32, kind="ExternalInput").ap()
    maskwh_in = dt("maskwh", [128, 128], F32, kind="ExternalInput").ap()
    onesc_in = dt("onesc", [128, 1], BF16, kind="ExternalInput").ap()
    onesr_in = dt("onesr", [1, 128], BF16, kind="ExternalInput").ap()
    out_hbm = dt("out", [TC, E], F32, kind="ExternalOutput").ap()
    skind = "ExternalOutput" if DEBUG else "Internal"
    xmid_d = dt("xmid_d", [TC, E], F32, kind=skind).ap()
    xl2t_d = dt("xl2t_d", [E, TC], BF16, kind=skind).ap()
    if DEBUG:
        dbg_wr = dt("dbg_wr", [128, 16 * NEXP], F32, kind="ExternalOutput").ap()
        dbg_ht = dt("dbg_ht", [128, 8 * 1024], BF16, kind="ExternalOutput").ap()
        dbg_acc = dt("dbg_acc", [128, 8 * 2048], F32, kind="ExternalOutput").ap()
        dbg_xl2h = dt("dbg_xl2h", [128, 16 * 1024], BF16, kind="ExternalOutput").ap()
        dbg_w = dt("dbg_w", [128, 16384], BF16, kind="ExternalOutput").ap()

    es = contextlib.ExitStack()
    sb = lambda n, s, d: es.enter_context(nc.sbuf_tensor(n, s, d))

    ident = sb("identt", [128, 128], BF16)
    maskw = sb("maskws", [128, 128], F32)
    maskc = sb("maskcs", [128, 128], F32)
    hbias = sb("hbiass", [128, 128], F32)
    maskwh = sb("maskwhs", [128, 128], F32)
    onesc = sb("onesc_s", [128, 1], BF16)
    onesr = sb("onesr_s", [1, 128], BF16)
    bq = sb("bq_s", [128, 64], F32)
    K2f = sb("K2f_s", [1, 4], F32)
    K2mb = sb("K2mb_s", [1, 4], BF16)
    k2t = sb("k2t_s", [1, 1], F32)
    wrt_s = sb("wrt_s", [128, 16 * NEXP], BF16)
    w_r = sb("w_r", [128, 16 * NEXP], F32)
    epsc = sb("epsc", [128, 1], F32)

    esA = contextlib.ExitStack()
    sba = lambda n, s, d: esA.enter_context(nc.sbuf_tensor(n, s, d))
    wq_sl = [sba(f"wq{i}", [128, 16 * 256], BF16) for i in range(2)]
    wo_sl = [sba(f"wos{i}", [128, 16 * 512], BF16) for i in range(2)]
    x_sl = [sba(f"x{i}", [128, E], F32) for i in range(2)]
    xlnbf = [sba(f"xlnbf{i}", [128, E], BF16) for i in range(2)]
    xT = sba("xT", [128, 16 * 512], BF16)
    qT = sba("qT", [128, 16 * 512], BF16)
    kT_sl = [sba(f"kT{i}", [128, KV * 512], BF16) for i in range(2)]
    v_sl = [sba(f"v{i}", [128, KV * 4 * 128], BF16) for i in range(2)]
    attS = sba("attS", [128, 4 * 640], BF16)
    attS2 = attS
    attT = attS
    xl2T = sba("xl2T", [128, 16 * 512], BF16)
    pair_ctr = [0]
    aoT = sba("aoT", [128, 16 * 512], BF16)
    xmid_sl = [sba(f"xmid{i}", [128, E], F32) for i in range(4)]
    cos_sl = [sba(f"cos{i}", [128, 512], F32) for i in range(2)]
    sin_sl = [sba(f"sin{i}", [128, 512], F32) for i in range(2)]
    ropeA = sba("ropeA", [128, 512], F32)
    ropeB = sba("ropeB", [128, 512], F32)

    stat6 = [sba(f"stat6_{i}", [128, 24], F32) for i in range(4)]
    mv = [sba(f"mv{i}", [128, 2], F32) for i in range(4)]
    istd = [sba(f"istd{i}", [128, 1], F32) for i in range(4)]
    nmi = [sba(f"nmi{i}", [128, 1], F32) for i in range(4)]
    lvar = [sba(f"lvar{i}", [128, 1], F32) for i in range(4)]
    nm4 = sba("nm4", [128, 4], F32)
    nms = sba("nms", [128, 4], F32)
    den = sba("den", [128, 4], F32)
    rden = sba("rden", [128, 4], F32)
    rl = [sba(f"rl{i}", [128, NEXP], F32) for i in range(4)]
    rz = [sba(f"rz{i}", [128, NEXP], F32) for i in range(4)]
    rind = [sba(f"rind{i}", [128, NEXP], F32) for i in range(4)]
    rm1 = [sba(f"rm1_{i}", [128, 1], F32) for i in range(4)]
    rm2 = [sba(f"rm2_{i}", [128, 1], F32) for i in range(4)]
    rtmp = [sba(f"rtmp{i}", [128, NEXP], F32) for i in range(4)]
    rden1 = [sba(f"rden1_{i}", [128, 1], F32) for i in range(4)]
    rrec = [sba(f"rrec{i}", [128, 1], F32) for i in range(4)]
    if DEBUG:
        print("SBUF remaining after phase-A allocs:", nc.sbuf_bytes_remaining)

    ps = contextlib.ExitStack()
    PS_p = [ps.enter_context(nc.psum_tensor(f"PS_p{i}", [128, 1536], F32))
            for i in range(2)]
    PS_t = ps.enter_context(nc.psum_tensor("PS_t", [128, 2048], BF16))

    P = Prog(nc)
    b_consts = Buf("consts")
    b_wrt = Buf("wrt")
    b_wr = Buf("w_r")
    b_wq = [Buf(f"wq{i}") for i in range(2)]
    b_wos = [Buf(f"wos{i}") for i in range(2)]
    b_x = [Buf(f"x{i}") for i in range(2)]
    b_xlnbf = [Buf(f"xlnbf{i}") for i in range(2)]
    b_xT = Buf("xT")
    b_qT = Buf("qT")
    b_kT = [Buf(f"kT{i}") for i in range(2)]
    b_v = [Buf(f"v{i}") for i in range(2)]
    b_attSh = [[Buf(f"attS{s}{h}") for h in range(2)] for s in range(2)]
    b_xl2T = Buf("xl2T")
    b_bq = Buf("bq")
    b_k2 = Buf("k2")
    b_aoT = Buf("aoT")
    b_xmid = [Buf(f"xmid{i}") for i in range(4)]
    b_cos = [Buf(f"cos{i}") for i in range(2)]
    b_ropeA = Buf("ropeA")
    b_ropeB = Buf("ropeB")

    b_stat = [Buf(f"stat{i}") for i in range(4)]
    b_lnsc = [Buf(f"lnsc{i}") for i in range(4)]
    b_attst = [Buf("attst0"), Buf("attst1")]
    b_rt = [Buf(f"router{i}") for i in range(4)]
    b_PSp = [Buf("PS_p0"), Buf("PS_p1")]
    b_PS_t = [Buf("PS_t0"), Buf("PS_t1")]
    b_xmid_d = Buf("xmid_d")
    b_xl2t_d = Buf("xl2t_d")

    P.step("sync", lambda E: E.dma_start(ident[:], ident_in), writes=[b_consts], dma="Ds")
    P.step("sync", lambda E: E.dma_start(maskw[:], maskw_in), writes=[b_consts], dma="Ds")
    P.step("sync", lambda E: E.dma_start(maskc[:], maskc_in), writes=[b_consts], dma="Ds")
    P.step("sync", lambda E: E.dma_start(hbias[:], hbias_in), writes=[b_consts], dma="Ds")
    P.step("sync", lambda E: E.dma_start(maskwh[:], maskwh_in), writes=[b_consts], dma="Ds")
    P.step("sync", lambda E: E.dma_start(onesc[:], onesc_in), writes=[b_consts], dma="Ds")
    P.step("sync", lambda E: E.dma_start(onesr[:], onesr_in), writes=[b_consts], dma="Ds")
    P.step("gpsimd", lambda E: E.dma_start(
        wrt_s.rearrange("p (c f) -> p c f", c=16), wrt), writes=[b_wrt], dma="Dg")
    P.step("vector", lambda E: E.memset(epsc[:], 1e-5), writes=[b_consts])
    P.step("vector", lambda E: E.memset(K2f[:], 0.0), writes=[b_k2])
    P.step("vector", lambda E: E.memset(K2mb[:], 0.0), writes=[b_k2])

    def ln_bf(tiles):
        for k, xbuf, bx, ob, bo in tiles:
            def bn(E, k=k, xbuf=xbuf):
                for i in range(4):
                    ins = E.bn_stats(stat6[k][:, i * 6:(i + 1) * 6],
                                     xbuf[:, i * 512:(i + 1) * 512])
                return ins
            P.step("vector", bn, reads=[bx], writes=[b_stat[k]])
        for k, xbuf, bx, ob, bo in tiles:
            P.step("vector", lambda E, k=k: E.bn_aggr(mv[k][:], stat6[k][:]),
                   reads=[b_stat[k]], writes=[b_lnsc[k]])
        for k, xbuf, bx, ob, bo in tiles:
            P.step("scalar", lambda E, k=k: E.activation(
                lvar[k][:], mv[k][:, 1:2], AF.Ln, bias=epsc[:]),
                reads=[b_lnsc[k], b_consts], writes=[b_lnsc[k]])
        for k, xbuf, bx, ob, bo in tiles:
            P.step("scalar", lambda E, k=k: E.activation(
                istd[k][:], lvar[k][:], AF.Exp, scale=-0.5),
                reads=[b_lnsc[k]], writes=[b_lnsc[k]])
        for k, xbuf, bx, ob, bo in tiles:
            P.step("vector", lambda E, k=k: E.tensor_scalar(
                nmi[k][:], mv[k][:, 0:1], -1.0, istd[k][:], OP.mult, OP.mult),
                reads=[b_lnsc[k]], writes=[b_lnsc[k]])
        for k, xbuf, bx, ob, bo in tiles:
            P.step("scalar", lambda E, k=k, xbuf=xbuf, ob=ob: E.activation(
                ob[:], xbuf[:], AF.Identity, bias=nmi[k][:], scale=istd[k][:]),
                reads=[bx, b_lnsc[k]], writes=[bo])

    def transpose16(srcbuf, b_src, dstbuf, b_dst, dst_col):
        for g in range(4):
            sl = g % 2
            off = sl * 1024

            def tr(E, g=g, off=off):
                for k in range(4):
                    mc = g * 4 + k
                    ins = E.transpose(PS_t[:, off + k * 128: off + (k + 1) * 128],
                                      srcbuf[:, mc * 128:(mc + 1) * 128], ident[:])
                return ins
            P.step("tensor", tr, reads=[b_src, b_consts], writes=[b_PS_t[sl]])
            dv = dstbuf.rearrange("p (c t) -> p c t", c=16)[:, g * 4:(g + 1) * 4,
                                                           dst_col:dst_col + 128]
            P.step("vector", lambda E, dv=dv, off=off: E.tensor_copy(
                dv, PS_t[:, off:off + 512].rearrange("p (c t) -> p c t", c=4)),
                reads=[b_PS_t[sl]], writes=[b_dst])

    def emit_ln1_qkv(s):
        owned = s >= 1
        sbp = s % 2
        P.step("sync", lambda E, s=s: E.dma_start(
            cos_sl[s % 2][:], cost[:, s * 512:(s + 1) * 512]),
            writes=[b_cos[s % 2]], dma="Ds")
        P.step("sync", lambda E, s=s: E.dma_start(
            sin_sl[s % 2][:], sint[:, s * 512:(s + 1) * 512]),
            writes=[b_cos[s % 2]], dma="Ds")

        for tp2 in range(2):
            tiles = []
            for xs in range(2):
                tt = tp2 * 2 + xs
                row0 = s * 512 + tt * 128
                P.step("sync", lambda E, row0=row0, xs=xs: E.dma_start(
                    x_sl[xs][:], x_in[row0:row0 + 128, :]), writes=[b_x[xs]], dma="Ds")
                tiles.append((xs, x_sl[xs], b_x[xs], xlnbf[xs], b_xlnbf[xs]))
            ln_bf(tiles)
            for xs in range(2):
                tt = tp2 * 2 + xs
                transpose16(xlnbf[xs], b_xlnbf[xs], xT, b_xT, tt * 128)

        ft_list = (list(range(16, 24)) + list(range(16))) if owned else list(range(16, 24))
        groups = [ft_list[i:i + 2] for i in range(0, len(ft_list), 2)]
        for gi, grp in enumerate(groups):
            wslot, bws = wq_sl[gi % 2], b_wq[gi % 2]
            g2 = grp[0] // 2
            nf = len(grp) * 128

            def wdma(E, g2=g2, wslot=wslot):
                return E.dma_start(
                    wslot[:, :].rearrange("p (c f) -> p c f", c=16),
                    wqkv[g2])
            P.step("gpsimd", wdma, writes=[bws], dma="Dg")
            for fi, ft in enumerate(grp):
                psf = PS_p[ft % 2]
                psq, bpsq = psf[:, 0:512], b_PSp[ft % 2]

                def mm(E, psq=psq, wslot=wslot, fi=fi, nf=nf):
                    for mc in range(16):
                        ins = E.matmul(psq[:, 0:512],
                                       wslot[:, mc * nf + fi * 128: mc * nf + (fi + 1) * 128],
                                       xT[:, mc * 512:(mc + 1) * 512],
                                       start=(mc == 0), stop=(mc == 15))
                    return ins
                P.step("tensor", mm, reads=[bws, b_xT], writes=[bpsq])
                if ft < 20:
                    sqc = xlnbf[1][:, 0:512]
                    if ft < 16:
                        P.step("scalar", lambda E, psq=psq, sqc=sqc: E.activation(
                            sqc, psq[:, 0:512], AF.Square),
                            reads=[bpsq], writes=[b_xlnbf[1]])
                        kv2 = ft // 4

                        def qn2(E, psf=psf, sqc=sqc, kv2=kv2):
                            for qt in range(4):
                                E.matmul(psf[:, 512 + qt:513 + qt],
                                         sqc[:, qt * 128:(qt + 1) * 128],
                                         onesc[:], start=True, stop=False)
                                ins = E.matmul(psf[:, 512 + qt:513 + qt],
                                               onesr[:], K2mb[:, kv2:kv2 + 1],
                                               start=False, stop=True)
                            return ins
                        P.step("tensor", qn2, reads=[b_xlnbf[1], b_consts, b_k2],
                               writes=[bpsq])
                        P.step("vector", lambda E, psf=psf, ft=ft: E.tensor_scalar(
                            bq[:, ft * 4:(ft + 1) * 4], psf[:, 512:516],
                            -0.5 * SCALE, -2.0, OP.mult, OP.add),
                            reads=[bpsq], writes=[b_bq])
                        dst, bdst = qT[:, ft * 512:(ft + 1) * 512], b_qT
                    else:
                        P.step("scalar", lambda E, psq=psq, sqc=sqc: E.activation(
                            sqc, psq[:, 0:512], AF.Square),
                            reads=[bpsq], writes=[b_xlnbf[1]])
                        P.step("tensor", lambda E, psf=psf, sqc=sqc: E.matmul(
                            psf[0:1, 512:1024], onesc[:], sqc, start=True, stop=True),
                            reads=[b_xlnbf[1], b_consts], writes=[bpsq])
                        kv2 = ft - 16
                        P.step("vector", lambda E, psf=psf: E.reduce_max(
                            k2t[:], psf[0:1, 512:1024], axis=AX),
                            reads=[bpsq], writes=[b_k2])
                        P.step("vector", lambda E, kv2=kv2: E.tensor_max(
                            K2f[:, kv2:kv2 + 1], K2f[:, kv2:kv2 + 1], k2t[:]),
                            reads=[b_k2], writes=[b_k2])
                        P.step("vector", lambda E, kv2=kv2: E.tensor_copy(
                            K2mb[:, kv2:kv2 + 1], K2f[:, kv2:kv2 + 1]),
                            reads=[b_k2], writes=[b_k2])
                        dst = kT_sl[sbp][:, (ft - 16) * 512:(ft - 15) * 512]
                        bdst = b_kT[sbp]
                    cs, sn, bc = cos_sl[s % 2], sin_sl[s % 2], b_cos[s % 2]
                    P.step("vector", lambda E, psq=psq, cs=cs: E.tensor_mul(
                        ropeA[:], psq[:, 0:512], cs[:]),
                        reads=[bpsq, bc], writes=[b_ropeA])

                    def bmul(E, psq=psq, sn=sn):
                        E.tensor_mul(ropeB[0:64, :], psq[64:128, 0:512], sn[0:64, :])
                        return E.tensor_mul(ropeB[64:128, :], psq[0:64, 0:512],
                                            sn[64:128, :])
                    P.step("vector", bmul, reads=[bpsq, bc], writes=[b_ropeB])
                    P.step("vector", lambda E, dst=dst: E.tensor_add(
                        dst, ropeA[:], ropeB[:]),
                        reads=[b_ropeA, b_ropeB], writes=[bdst])
                else:
                    kv = ft - 20
                    vtmp = xlnbf[0][:, 0:512]
                    P.step("vector", lambda E, psq=psq, vtmp=vtmp: E.tensor_copy(
                        vtmp, psq[:, 0:512]), reads=[bpsq], writes=[b_xlnbf[0]])

                    def vtr(E, vtmp=vtmp):
                        for u in range(4):
                            ins = E.transpose(PS_t[:, u * 128:(u + 1) * 128],
                                              vtmp[:, u * 128:(u + 1) * 128], ident[:])
                        return ins
                    P.step("tensor", vtr, reads=[b_xlnbf[0], b_consts], writes=[b_PS_t[0]])
                    P.step("vector", lambda E, kv=kv, sbp=sbp: E.tensor_copy(
                        v_sl[sbp][:, kv * 512:(kv + 1) * 512], PS_t[:, 0:512]),
                        reads=[b_PS_t[0]], writes=[b_v[sbp]])


    def emit_attention(s):
        def att_scores(pi, QT, qtl, kv, hp):
            sl = pi % 2
            PS = PS_p[sl]
            bPS = b_PSp[sl]
            r = QT % 4
            s1 = (QT // 4) % 2
            len1 = (4 - r) * 128

            def scores(E, QT=QT, qtl=qtl, kv=kv, hp=hp, PS=PS, r=r, s1=s1,
                       len1=len1):
                for hhp in range(2):
                    h = kv * 4 + 2 * hp + hhp
                    qsl = qT[:, h * 512 + qtl * 128: h * 512 + (qtl + 1) * 128]
                    base = hhp * 640
                    bs = 512 if hhp == 0 else 384
                    cuts = sorted({0, len1, bs, 640})
                    for a, b2 in zip(cuts, cuts[1:]):
                        if a < len1:
                            ksl = kT_sl[s1][:, kv * 512 + r * 128 + a:
                                            kv * 512 + r * 128 + b2]
                        else:
                            ksl = kT_sl[1 - s1][:, kv * 512 + a - len1:
                                                kv * 512 + b2 - len1]
                        ins = E.matmul(PS[:, base + a:base + b2], qsl, ksl,
                                       start=True, stop=True)
                return ins
            P.step("tensor", scores, reads=[b_qT, b_kT[0], b_kT[1]],
                   writes=[bPS])

            def maskadd(E, QT=QT, PS=PS):
                m0 = maskwh if QT <= 3 else maskw
                for hhp in range(2):
                    base = hhp * 640
                    E.tensor_add(PS[:, base:base + 128],
                                 PS[:, base:base + 128], m0[:])
                    for j in range(1, 4):
                        if QT + j <= 3:
                            E.tensor_add(PS[:, base + j * 128:base + (j + 1) * 128],
                                         PS[:, base + j * 128:base + (j + 1) * 128],
                                         hbias[:])
                    ins = E.tensor_add(PS[:, base + 512:base + 640],
                                       PS[:, base + 512:base + 640], maskc[:])
                return ins
            P.step("vector", maskadd, reads=[b_consts], writes=[bPS])

        def att_frontA(pi, QT, qtl, kv, hp):
            sl = pi % 2
            PS = PS_p[sl]
            bPS = b_PSp[sl]
            aoff = sl * 1280
            coff = sl * 2
            bst = b_attst[sl]
            bSh = b_attSh[sl]

            def exps(E, PS=PS, aoff=aoff, coff=coff, QT=QT, qtl=qtl, kv=kv, hp=hp):
                for hhp in range(2):
                    h = kv * 4 + 2 * hp + hhp
                    ins = E.activation(
                        attS[:, aoff + hhp * 640: aoff + (hhp + 1) * 640],
                        PS[:, hhp * 640:(hhp + 1) * 640], AF.Exp,
                        bias=bq[:, h * 4 + qtl: h * 4 + qtl + 1], scale=SCALE,
                        accum_out=den[:, coff + hhp:coff + hhp + 1])
                return ins
            P.step("scalar", exps, reads=[bPS, b_bq, bst],
                   writes=[bSh[0], bSh[1], bst])

        def att_frontC(pi, QT, qtl, kv, hp):
            sl = pi % 2
            aoff = sl * 1280
            coff = sl * 2
            bst = b_attst[sl]
            bSh = b_attSh[sl]
            P.step("vector", lambda E, coff=coff: E.reciprocal(
                rden[:, coff:coff + 2], den[:, coff:coff + 2]),
                reads=[bst], writes=[bst])

            def attscale(E, aoff=aoff, coff=coff):
                for hhp in range(2):
                    ins = E.tensor_scalar_mul(
                        attS2[:, aoff + hhp * 640: aoff + (hhp + 1) * 640],
                        attS[:, aoff + hhp * 640: aoff + (hhp + 1) * 640],
                        rden[:, coff + hhp:coff + hhp + 1])
                return ins
            P.step("vector", attscale, reads=[bst],
                   writes=[bSh[0], bSh[1]])

        def att_back(pi, QT, qtl, kv, hp):
            sl = pi % 2
            PS = PS_p[sl]
            bPS = b_PSp[sl]
            aoff = sl * 1280
            bSh = b_attSh[sl]
            for hhp in range(2):
                ti = hhp
                toff = ti * 1024

                def tr(E, aoff=aoff, hhp=hhp, toff=toff):
                    for j in range(5):
                        ins = E.transpose(
                            PS_t[:, toff + j * 128: toff + (j + 1) * 128],
                            attS2[:, aoff + hhp * 640 + j * 128:
                                  aoff + hhp * 640 + (j + 1) * 128],
                            ident[:])
                    return ins
                P.step("tensor", tr, reads=[bSh[hhp], b_consts], writes=[b_PS_t[ti]])
                ev_eng = "vector" if hhp == 0 else "scalar"

                def evacT(E, aoff=aoff, hhp=hhp, toff=toff):
                    if hhp == 0:
                        return E.tensor_copy(
                            attT[:, aoff + hhp * 640: aoff + (hhp + 1) * 640],
                            PS_t[:, toff:toff + 640])
                    return E.copy(
                        attT[:, aoff + hhp * 640: aoff + (hhp + 1) * 640],
                        PS_t[:, toff:toff + 640])
                P.step(ev_eng, evacT, reads=[b_PS_t[ti]], writes=[bSh[hhp]])
            for hhp in range(2):
                def av(E, QT=QT, kv=kv, hhp=hhp, aoff=aoff, PS=PS):
                    for j in range(5):
                        J = QT + j
                        vsl = v_sl[(J // 4) % 2][:, (kv * 4 + J % 4) * 128:
                                                 (kv * 4 + J % 4 + 1) * 128]
                        ins = E.matmul(
                            PS[:, 1280 + hhp * 128:1280 + (hhp + 1) * 128],
                            vsl,
                            attT[:, aoff + hhp * 640 + j * 128:
                                 aoff + hhp * 640 + (j + 1) * 128],
                            start=(j == 0), stop=(j == 4))
                    return ins
                P.step("tensor", av, reads=[bSh[hhp], b_v[0], b_v[1]], writes=[bPS])

            h0 = kv * 4 + 2 * hp
            dv = aoT.rearrange("p (h t) -> p h t", h=16)[:, h0:h0 + 2,
                                                         qtl * 128:(qtl + 1) * 128]
            P.step("vector", lambda E, dv=dv, PS=PS: E.tensor_copy(
                dv, PS[:, 1280:1536].rearrange("p (h t) -> p h t", h=2)),
                reads=[bPS], writes=[b_aoT])

        pairs = [((s - 1) * 4 + qtl, qtl, kv, hp)
                 for qtl in range(4) for kv in range(KV) for hp in range(2)]
        prev = None
        for args in pairs:
            pi = pair_ctr[0]
            pair_ctr[0] += 1
            att_scores(pi, *args)
            att_frontA(pi, *args)
            if prev is not None:
                att_back(*prev)
            att_frontC(pi, *args)
            prev = (pi, *args)
        att_back(*prev)

    def emit_wo_ln2(s):
        for mq in range(4):
            wsl, bwsl = wo_sl[mq % 2], b_wos[mq % 2]

            def wodma(E, mq=mq, wsl=wsl):
                return E.dma_start(
                    wsl[:, :].rearrange("p (c f) -> p c f", c=16),
                    wo_in[mq])
            P.step("gpsimd", wodma, writes=[bwsl], dma="Dg")

            def womm(E, wsl=wsl):
                for tt in range(4):
                    PS = PS_p[tt // 2]
                    for fc in range(16):
                        ins = E.matmul(PS[:, (tt % 2) * 512:(tt % 2 + 1) * 512],
                                       aoT[:, fc * 512 + tt * 128:
                                           fc * 512 + (tt + 1) * 128],
                                       wsl[:, fc * 512:(fc + 1) * 512],
                                       start=(fc == 0), stop=(fc == 15))
                return ins
            P.step("tensor", womm, reads=[b_aoT, bwsl],
                   writes=[b_PSp[0], b_PSp[1]])
            for tt in range(4):
                eng = "vector" if tt % 2 == 0 else "scalar"

                def evac(E, mq=mq, tt=tt, eng=eng):
                    PSsl = PS_p[tt // 2][:, (tt % 2) * 512:(tt % 2 + 1) * 512]
                    dst = xmid_sl[tt][:, mq * 512:(mq + 1) * 512]
                    if eng == "vector":
                        return E.tensor_copy(dst, PSsl)
                    return E.copy(dst, PSsl)
                P.step(eng, evac, reads=[b_PSp[tt // 2]], writes=[b_xmid[tt]])
        for tt in range(4):
            row0 = s * 512 + tt * 128
            xs = tt % 2
            xm, bxm = xmid_sl[tt], b_xmid[tt]
            if tt % 2 == 0:
                for xs2 in range(2):
                    P.step("sync", lambda E, row0=s * 512 + (tt + xs2) * 128, xs2=xs2:
                           E.dma_start(x_sl[xs2][:], x_in[row0:row0 + 128, :]),
                           writes=[b_x[xs2]], dma="Ds")
            P.step("vector", lambda E, xm=xm, xs=xs: E.tensor_add(
                xm[:], xm[:], x_sl[xs][:]),
                reads=[b_x[xs]], writes=[bxm])
        for tt in range(4):
            t_owned = (s - 1) * 4 + tt
            xm, bxm = xmid_sl[tt], b_xmid[tt]
            P.step("sync", lambda E, t_owned=t_owned, xm=xm: E.dma_start(
                xmid_d[t_owned * 128:(t_owned + 1) * 128, :], xm[:]),
                reads=[bxm], writes=[b_xmid_d], dma="Ds")

        ln_bf([(tt, xmid_sl[tt], b_xmid[tt], xlnbf[tt % 2], b_xlnbf[tt % 2])
               for tt in (0, 1)])
        for tt in (0, 1):
            transpose16(xlnbf[tt % 2], b_xlnbf[tt % 2], xl2T, b_xl2T, tt * 128)
        ln_bf([(tt, xmid_sl[tt], b_xmid[tt], xlnbf[tt % 2], b_xlnbf[tt % 2])
               for tt in (2, 3)])
        for tt in (2, 3):
            transpose16(xlnbf[tt % 2], b_xlnbf[tt % 2], xl2T, b_xl2T, tt * 128)

    def emit_router_spill(s):
        for tt in range(4):
            def rmm(E, tt=tt):
                for mc in range(16):
                    ins = E.matmul(PS_p[1][:, tt * 32:tt * 32 + NEXP],
                                   xl2T[:, mc * 512 + tt * 128:
                                        mc * 512 + (tt + 1) * 128],
                                   wrt_s[:, mc * NEXP:(mc + 1) * NEXP],
                                   start=(mc == 0), stop=(mc == 15))
                return ins
            P.step("tensor", rmm, reads=[b_xl2T, b_wrt], writes=[b_PSp[1]])
            P.step("vector", lambda E, tt=tt: E.tensor_copy(
                rl[tt][:], PS_p[1][:, tt * 32:tt * 32 + NEXP]),
                reads=[b_PSp[1]], writes=[b_rt[tt]])
        for tt in range(4):
            P.step("vector", lambda E, tt=tt: E.reduce_max(
                rm1[tt][:], rl[tt][:], axis=AX, negate=True),
                reads=[b_rt[tt]], writes=[b_rt[tt]])
        for tt in range(4):
            P.step("vector", lambda E, tt=tt: E.tensor_scalar(
                rz[tt][:], rl[tt][:], -1.0, rm1[tt][:], OP.mult, OP.is_le),
                reads=[b_rt[tt]], writes=[b_rt[tt]])
        for tt in range(4):
            P.step("vector", lambda E, tt=tt: E.scalar_tensor_tensor(
                rtmp[tt][:], rz[tt][:], -1e30, rl[tt][:], OP.mult, OP.add),
                reads=[b_rt[tt]], writes=[b_rt[tt]])
        for tt in range(4):
            P.step("vector", lambda E, tt=tt: E.reduce_max(
                rm2[tt][:], rtmp[tt][:], axis=AX, negate=True),
                reads=[b_rt[tt]], writes=[b_rt[tt]])
        for tt in range(4):
            P.step("vector", lambda E, tt=tt: E.tensor_scalar(
                rind[tt][:], rl[tt][:], -1.0, rm2[tt][:], OP.mult, OP.is_le),
                reads=[b_rt[tt]], writes=[b_rt[tt]])
        for tt in range(4):
            P.step("scalar", lambda E, tt=tt: E.activation(
                rz[tt][:], rl[tt][:], AF.Exp, bias=rm1[tt][:]),
                reads=[b_rt[tt]], writes=[b_rt[tt]])
        for tt in range(4):
            P.step("vector", lambda E, tt=tt: E.scalar_tensor_tensor(
                rtmp[tt][:], rz[tt][:], 1.0, rind[tt][:], OP.mult, OP.mult,
                accum_out=rden1[tt][:]),
                reads=[b_rt[tt]], writes=[b_rt[tt]])
        for tt in range(4):
            P.step("vector", lambda E, tt=tt: E.reciprocal(
                rrec[tt][:], rden1[tt][:]),
                reads=[b_rt[tt]], writes=[b_rt[tt]])
        for tt in range(4):
            t_owned = (s - 1) * 4 + tt
            P.step("vector", lambda E, tt=tt, t_owned=t_owned: E.tensor_scalar_mul(
                w_r[:, t_owned * NEXP:(t_owned + 1) * NEXP], rtmp[tt][:], rrec[tt][:]),
                reads=[b_rt[tt]], writes=[b_wr])


        def xl2spill(E, s=s):
            return [E.dma_start(
                xl2t_d[g * 512:(g + 1) * 512, (s - 1) * 512:s * 512].rearrange(
                    "(c p) t -> p c t", p=128),
                xl2T[:, g * 4 * 512:(g + 1) * 4 * 512].rearrange(
                    "p (c t) -> p c t", c=4)) for g in range(4)]
        P.step("sync", xl2spill, reads=[b_xl2T], writes=[b_xl2t_d], dma="Ds", ndma=4)


    emit_ln1_qkv(0)
    emit_ln1_qkv(1)
    for s in range(1, 5):
        emit_attention(s)
        emit_wo_ln2(s)
        if s < 4:
            emit_ln1_qkv(s + 1)
        emit_router_spill(s)

    P.barrier()
    ps.close()
    esA.close()
    PGU = es.enter_context(nc.psum_tensor("PGU", [128, 2048], F32))
    PD = es.enter_context(nc.psum_tensor("PD", [128, 2048], F32))
    b_PG, b_PU = Buf("PG"), Buf("PU")
    b_PD = [Buf("PDa"), Buf("PDb")]

    acc = sb("acc", [128, 8 * 2048], F32)
    xl2h = sb("xl2h", [128, 16 * 1024], BF16)
    wsl2 = [sb(f"moew{i}", [128, 16384], BF16) for i in range(2)]
    hT = sb("hT", [128, 8 * 1024], BF16)
    sg = sb("sg", [128, 1024], F32)
    b_acc_t = [Buf(f"acc{i}") for i in range(8)]
    b_xl2h = Buf("xl2h")
    b_w2 = [Buf(f"moew{i}") for i in range(2)]
    b_hT = Buf("hT")
    b_sg = Buf("sg")
    if DEBUG:
        print("SBUF remaining after MoE allocs:", nc.sbuf_bytes_remaining)

    st_tok = []

    def emit_accload(half):
        for u in range(2):
            def accload(E, half=half, u=u):
                return [E.dma_start(
                    acc[:, u * 4 * 2048:(u + 1) * 4 * 2048].rearrange(
                        "p (c m) -> p c m", c=4),
                    xmid_d[half * 1024 + u * 512: half * 1024 + (u + 1) * 512,
                           :].rearrange("(c p) m -> p c m", p=128))]
            P.step("sync", accload, reads=[b_xmid_d],
                   writes=[b_acc_t[4 * u + i] for i in range(4)], dma="Ds", ndma=1)

    def emit_xl2load(half):
        def xl2load(E, half=half):
            return [E.dma_start(
                xl2h[:, g * 4 * 1024:(g + 1) * 4 * 1024].rearrange(
                    "p (c t) -> p c t", c=4),
                xl2t_d[g * 512:(g + 1) * 512,
                       half * 1024:(half + 1) * 1024].rearrange(
                    "(c p) t -> p c t", p=128)) for g in range(4)]
        P.step("sync", xl2load, reads=[b_xl2t_d], writes=[b_xl2h], dma="Ds", ndma=4)

    def emit_gu_dma(e, fh, sl):
        wsl, bw = wsl2[sl], b_w2[sl]

        def both(E, e=e, fh=fh, wsl=wsl):
            return E.dma_start(
                wsl[:, :].rearrange("p (g c f) -> p g c f", g=4, c=4),
                wgu_in[e, fh])
        P.step("gpsimd", both, writes=[bw], dma="Dg")
        return sl

    def emit_wd_dma(e, sl):
        wsl, bw = wsl2[sl], b_w2[sl]

        def wddma(E, e=e, wsl=wsl):
            return E.dma_start(
                wsl[:, :].rearrange("p (c m) -> p c m", c=8),
                wd_in[e])
        P.step("gpsimd", wddma, writes=[bw], dma="Dg")
        return sl

    def emit_gu_chunk(e, fh, fc, sl):
        hc = fh * 4 + fc
        wsl, bw = wsl2[sl], b_w2[sl]

        def gummG(E, wsl=wsl, fc=fc):
            for mc in range(16):
                for h2 in range(2):
                    ins = E.matmul(
                        PGU[:, h2 * 512:(h2 + 1) * 512],
                        wsl[:, mc * 1024 + fc * 128: mc * 1024 + (fc + 1) * 128],
                        xl2h[:, mc * 1024 + h2 * 512: mc * 1024 + (h2 + 1) * 512],
                        start=(mc == 0), stop=(mc == 15))
            return ins
        P.step("tensor", gummG, reads=[bw, b_xl2h], writes=[b_PG])

        def gummU(E, wsl=wsl, fc=fc):
            for mc in range(16):
                for h2 in range(2):
                    ins = E.matmul(
                        PGU[:, 1024 + h2 * 512:1024 + (h2 + 1) * 512],
                        wsl[:, mc * 1024 + 512 + fc * 128:
                            mc * 1024 + 512 + (fc + 1) * 128],
                        xl2h[:, mc * 1024 + h2 * 512: mc * 1024 + (h2 + 1) * 512],
                        start=(mc == 0), stop=(mc == 15))
            return ins
        P.step("tensor", gummU, reads=[bw, b_xl2h], writes=[b_PU])
        P.step("scalar", lambda E: E.activation(sg[:], PGU[:, 0:1024], AF.Silu),
               reads=[b_PG], writes=[b_sg])
        P.step("vector", lambda E, hc=hc: E.tensor_mul(
            hT[:, hc * 1024:(hc + 1) * 1024], sg[:], PGU[:, 1024:2048]),
            reads=[b_sg, b_PU], writes=[b_hT])

    def emit_down(half, e, sl):
        wsl, bw = wsl2[sl], b_w2[sl]
        for tt in range(8):
            t_owned = half * 8 + tt
            for mh in range(2):
                def dmm(E, wsl=wsl, tt=tt, mh=mh):
                    for hc in range(8):
                        for m in range(2):
                            ins = E.matmul(
                                PD[:, mh * 1024 + m * 512: mh * 1024 + (m + 1) * 512],
                                hT[:, hc * 1024 + tt * 128: hc * 1024 + (tt + 1) * 128],
                                wsl[:, hc * 2048 + mh * 1024 + m * 512:
                                    hc * 2048 + mh * 1024 + (m + 1) * 512],
                                start=(hc == 0), stop=(hc == 7))
                    return ins
                P.step("tensor", dmm, reads=[bw, b_hT], writes=[b_PD[mh]])
                P.step("vector", lambda E, e=e, tt=tt, mh=mh, t_owned=t_owned:
                       E.scalar_tensor_tensor(
                           acc[:, tt * 2048 + mh * 1024: tt * 2048 + (mh + 1) * 1024],
                           PD[:, mh * 1024:(mh + 1) * 1024],
                           w_r[:, t_owned * NEXP + e: t_owned * NEXP + e + 1],
                           acc[:, tt * 2048 + mh * 1024: tt * 2048 + (mh + 1) * 1024],
                           OP.mult, OP.add),
                       reads=[b_PD[mh], b_wr, b_acc_t[tt]], writes=[b_acc_t[tt]])
            if e == NEXP - 1:
                def sttt(E, half=half, tt=tt):
                    return [E.dma_start(
                        out_hbm[half * 1024 + tt * 128: half * 1024 + (tt + 1) * 128, :],
                        acc[:, tt * 2048:(tt + 1) * 2048])]
                st_tok.append(P.step("sync", sttt, reads=[b_acc_t[tt]],
                                     dma="Ds", ndma=1))

    seq = [(h, e) for h in range(2) for e in range(NEXP)]
    emit_accload(0)
    emit_xl2load(0)
    s_fh0 = emit_gu_dma(0, 0, 0)
    s_fh1 = emit_gu_dma(0, 1, 1)
    for idx, (half, e) in enumerate(seq):
        nxt = seq[idx + 1] if idx + 1 < len(seq) else None
        for fc in range(4):
            emit_gu_chunk(e, 0, fc, s_fh0)
        s_wd = emit_wd_dma(e, s_fh0)
        for fc in range(4):
            emit_gu_chunk(e, 1, fc, s_fh1)
        if nxt is not None:
            if nxt == (1, 0):
                emit_xl2load(1)
            s_fh0n = emit_gu_dma(nxt[1], 0, s_fh1)
        emit_down(half, e, s_wd)
        if nxt is not None:
            if nxt == (1, 0):
                emit_accload(1)
            s_fh1n = emit_gu_dma(nxt[1], 1, s_wd)
            s_fh0, s_fh1 = s_fh0n, s_fh1n



    for t in st_tok:
        for pair in t:
            P._emit_wait("sync", pair)

    with nc.Block() as block:
        P.finalize(block)
    es.close()
    return nc


_NC_CACHE = []


def _rope_tables(pos):
    inv = (1.0 / (THETA ** (np.arange(0, D, 2, dtype=np.float32) / D))).astype(np.float32)
    ang = pos[None, :].astype(np.float32) * inv[:, None]
    cos = np.cos(ang).astype(np.float32)
    sin = np.sin(ang).astype(np.float32)
    cost = np.concatenate([cos, cos], axis=0)
    sint = np.concatenate([-sin, sin], axis=0)
    return np.ascontiguousarray(cost), np.ascontiguousarray(sint)


def kernel(L0_attn_x, L0_attn_w_qkv, L0_attn_w_o, L0_ffn_router_gate,
           L0_ffn_expert_gate_up, L0_ffn_expert_down):
    x = np.ascontiguousarray(np.asarray(L0_attn_x, dtype=np.float32))
    wqkv = np.asarray(L0_attn_w_qkv, dtype=np.float32)
    wo = np.asarray(L0_attn_w_o, dtype=np.float32)
    wr = np.asarray(L0_ffn_router_gate, dtype=np.float32)
    wgu = np.asarray(L0_ffn_expert_gate_up, dtype=np.float32)
    wd = np.asarray(L0_ffn_expert_down, dtype=np.float32)
    wg = wgu[:, :, 0::2]
    wu = wgu[:, :, 1::2]
    wqkv_t = np.ascontiguousarray(
        wqkv.reshape(16, 128, 12, 256).transpose(2, 1, 0, 3))
    wo_t = np.ascontiguousarray(
        wo.reshape(16, 128, 4, 512).transpose(2, 1, 0, 3))
    wrt_t = np.ascontiguousarray(wr.T.reshape(16, 128, NEXP).transpose(1, 0, 2))
    wg_t = wg.reshape(NEXP, 4, 4, 128, 2, 512).transpose(0, 4, 3, 1, 2, 5)
    wu_t = wu.reshape(NEXP, 4, 4, 128, 2, 512).transpose(0, 4, 3, 1, 2, 5)
    wgu_t = np.ascontiguousarray(np.concatenate([wg_t, wu_t], axis=-1))
    wd_t = np.ascontiguousarray(
        wd.reshape(NEXP, 8, 128, 2048).transpose(0, 2, 1, 3))

    ident = np.eye(128, dtype=ml_dtypes.bfloat16)
    r = np.arange(128)
    maskw = np.where(r[None, :] > r[:, None], 0.0, NEG).astype(np.float32)
    maskc = np.where(r[None, :] <= r[:, None], 0.0, NEG).astype(np.float32)
    in_maps = []
    for c in range(8):
        b, o = c // 2, (c % 2) * TC
        if o == 0:
            halo = np.zeros((LH, E), dtype=np.float32)
            hb = np.full((128, 128), NEG, dtype=np.float32)
        else:
            halo = x[b, o - LH:o]
            hb = np.zeros((128, 128), dtype=np.float32)
        xs = np.ascontiguousarray(np.concatenate([halo, x[b, o:o + TC]], axis=0))
        pos = np.maximum(np.arange(o - LH, o + TC), 0)
        cost, sint = _rope_tables(pos)
        in_maps.append({
            "x": xs, "wqkv": wqkv_t, "wo": wo_t, "wrt": wrt_t, "wgu": wgu_t,
            "wd": wd_t, "cost": cost, "sint": sint, "ident": ident,
            "maskw": maskw, "maskc": maskc, "hbias": hb,
            "maskwh": (maskw + hb).astype(np.float32),
            "onesc": np.ones((128, 1), dtype=ml_dtypes.bfloat16),
            "onesr": np.ones((1, 128), dtype=ml_dtypes.bfloat16),
        })

    if not _NC_CACHE:
        _NC_CACHE.append(build_nc())
    nc = _NC_CACHE[0]
    res = run_bass_kernel_spmd(nc, in_maps, core_ids=list(range(8)))
    out = np.empty((B, S, E), dtype=np.float32)
    for c in range(8):
        b, o = c // 2, (c % 2) * TC
        out[b, o:o + TC] = res.results[c]["out"]
    return out


# version 20 (final)
# speedup vs baseline: 1.0146x; 1.0146x over previous
import contextlib
import numpy as np
import ml_dtypes
import concourse.bass as bass
import concourse.mybir as mybir
from concourse.bass_utils import run_bass_kernel_spmd

F32 = mybir.dt.float32
BF16 = mybir.dt.bfloat16
AF = mybir.ActivationFunctionType
AX = mybir.AxisListType.X
OP = mybir.AluOpType

DEBUG = False

B, S, E = 4, 4096, 2048
H, KV, D = 16, 4, 128
THETA = 1e6
NEXP, INTER = 8, 1024
TC = 2048
LH = 512
LTOT = TC + LH
SCALE = 1.0 / float(np.sqrt(np.float32(D)))
NEG = -1e9


class Buf:
    __slots__ = ("w", "r", "name")

    def __init__(self, name=""):
        self.w = None
        self.r = []
        self.name = name


class Prog:
    CENG = ("tensor", "vector", "scalar", "gpsimd")
    NDG = 6
    NDS = 4

    def __init__(self, nc):
        self.nc = nc
        self.q = {e: [] for e in ("tensor", "vector", "scalar", "gpsimd", "sync")}
        self.sem = {}
        self.cnt = {}
        self.waited = {e: {} for e in self.q}
        names = ["Ep", "Ev", "Ea", "Eg", "Es"]
        names += [f"Dg{i}" for i in range(self.NDG)]
        names += [f"Ds{i}" for i in range(self.NDS)]
        for name in names:
            self.sem[name] = nc.alloc_semaphore(name)
            self.cnt[name] = 0
        self.esem = {"tensor": "Ep", "vector": "Ev", "scalar": "Ea",
                     "gpsimd": "Eg", "sync": "Es"}
        self.rr = {"Dg": 0, "Ds": 0}

    def _emit_wait(self, eng, pair):
        sname, val = pair
        if val <= 0 or self.waited[eng].get(sname, 0) >= val:
            return
        self.waited[eng][sname] = val
        h = self.sem[sname]
        self.q[eng].append(lambda E, h=h, v=val: E.wait_ge(h, v))

    def step(self, eng, fn, reads=(), writes=(), waits=(), dma=None, ndma=1):
        need = []
        for b in reads:
            if b.w is not None:
                need.extend(b.w)
        for b in writes:
            if b.w is not None:
                need.extend(b.w)
            for t in b.r:
                need.extend(t)
        for t in waits:
            need.extend(t)
        if dma is not None:
            assign = []
            for _ in range(ndma):
                i = self.rr[dma]
                self.rr[dma] = (i + 1) % (self.NDG if dma == "Dg" else self.NDS)
                sname = f"{dma}{i}"
                need.append((sname, self.cnt[sname]))
                assign.append(sname)
        best = {}
        for sname, val in need:
            if best.get(sname, -1) < val:
                best[sname] = val
        for sname in sorted(best):
            self._emit_wait(eng, (sname, best[sname]))
        mysem = self.esem[eng]
        if dma is not None:
            pairs = []
            for sname in assign:
                self.cnt[sname] += 16
                pairs.append((sname, self.cnt[sname]))
            tok = tuple(pairs)
            hs = [self.sem[s] for s in assign]

            def run(E, fn=fn, hs=hs):
                r = fn(E)
                if not isinstance(r, list):
                    r = [r]
                for ins, h in zip(r, hs, strict=True):
                    ins.then_inc(h, 16)
        else:
            self.cnt[mysem] += 1
            tok = ((mysem, self.cnt[mysem]),)
            h = self.sem[mysem]

            def run(E, fn=fn, h=h):
                fn(E).then_inc(h, 1)
        self.q[eng].append(run)
        for b in reads:
            b.r.append(tok)
        for b in writes:
            b.w = tok
            b.r = []
        return tok

    def barrier(self):
        pairs = [(s, self.cnt[s]) for s in self.cnt]
        for e in self.q:
            for t in pairs:
                self._emit_wait(e, t)

    def finalize(self, block):
        for ename, meth in (("tensor", block.tensor), ("vector", block.vector),
                            ("scalar", block.scalar), ("gpsimd", block.gpsimd),
                            ("sync", block.sync)):
            ops = self.q[ename]

            def body(E, ops=ops):
                for f in ops:
                    f(E)
            meth(body)


def build_nc():
    nc = bass.Bass("TRN2", target_bir_lowering=False, debug=False)
    dt = nc.dram_tensor
    x_in = dt("x", [LTOT, E], F32, kind="ExternalInput").ap()
    wqkv = dt("wqkv", [E, (H + 2 * KV) * D], F32, kind="ExternalInput").ap()
    wo_in = dt("wo", [H * D, E], F32, kind="ExternalInput").ap()
    wrt = dt("wrt", [E, NEXP], F32, kind="ExternalInput").ap()
    wgu_in = dt("wgu", [NEXP, 2, 128, 4, 4, 1024], F32, kind="ExternalInput").ap()
    wd_in = dt("wd", [NEXP, 128, 8, 2048], F32, kind="ExternalInput").ap()
    cost = dt("cost", [128, LTOT], F32, kind="ExternalInput").ap()
    sint = dt("sint", [128, LTOT], F32, kind="ExternalInput").ap()
    ident_in = dt("ident", [128, 128], BF16, kind="ExternalInput").ap()
    maskw_in = dt("maskw", [128, 128], F32, kind="ExternalInput").ap()
    maskc_in = dt("maskc", [128, 128], F32, kind="ExternalInput").ap()
    hbias_in = dt("hbias", [128, 128], F32, kind="ExternalInput").ap()
    maskwh_in = dt("maskwh", [128, 128], F32, kind="ExternalInput").ap()
    maskwb_in = dt("maskwb", [128, 128], BF16, kind="ExternalInput").ap()
    maskcb_in = dt("maskcb", [128, 128], BF16, kind="ExternalInput").ap()
    maskwhb_in = dt("maskwhb", [128, 128], BF16, kind="ExternalInput").ap()
    hbiasb_in = dt("hbiasb", [128, 128], BF16, kind="ExternalInput").ap()
    onesc_in = dt("onesc", [128, 1], BF16, kind="ExternalInput").ap()
    onesr_in = dt("onesr", [1, 128], BF16, kind="ExternalInput").ap()
    out_hbm = dt("out", [TC, E], F32, kind="ExternalOutput").ap()
    skind = "ExternalOutput" if DEBUG else "Internal"
    xmid_d = dt("xmid_d", [TC, E], F32, kind=skind).ap()
    xl2t_d = dt("xl2t_d", [E, TC], BF16, kind=skind).ap()
    if DEBUG:
        dbg_wr = dt("dbg_wr", [128, 16 * NEXP], F32, kind="ExternalOutput").ap()
        dbg_ht = dt("dbg_ht", [128, 8 * 1024], BF16, kind="ExternalOutput").ap()
        dbg_acc = dt("dbg_acc", [128, 8 * 2048], F32, kind="ExternalOutput").ap()
        dbg_xl2h = dt("dbg_xl2h", [128, 16 * 1024], BF16, kind="ExternalOutput").ap()
        dbg_w = dt("dbg_w", [128, 16384], BF16, kind="ExternalOutput").ap()

    es = contextlib.ExitStack()
    sb = lambda n, s, d: es.enter_context(nc.sbuf_tensor(n, s, d))

    ident = sb("identt", [128, 128], BF16)
    maskw = sb("maskws", [128, 128], F32)
    maskc = sb("maskcs", [128, 128], F32)
    hbias = sb("hbiass", [128, 128], F32)
    maskwh = sb("maskwhs", [128, 128], F32)
    maskwb = sb("maskwb_s", [128, 128], BF16)
    maskcb = sb("maskcb_s", [128, 128], BF16)
    maskwhb = sb("maskwhb_s", [128, 128], BF16)
    hbiasb = sb("hbiasb_s", [128, 128], BF16)
    onesc = sb("onesc_s", [128, 1], BF16)
    onesr = sb("onesr_s", [1, 128], BF16)
    bq = sb("bq_s", [128, 64], F32)
    K2f = sb("K2f_s", [1, 4], F32)
    K2mb = sb("K2mb_s", [1, 4], BF16)
    k2t = sb("k2t_s", [1, 1], F32)
    wrt_s = sb("wrt_s", [128, 16 * NEXP], BF16)
    w_r = sb("w_r", [128, 16 * NEXP], F32)
    epsc = sb("epsc", [128, 1], F32)

    esA = contextlib.ExitStack()
    sba = lambda n, s, d: esA.enter_context(nc.sbuf_tensor(n, s, d))
    wq_sl = [sba(f"wq{i}", [128, 16 * 256], BF16) for i in range(2)]
    wo_sl = [sba(f"wos{i}", [128, 16 * 512], BF16) for i in range(2)]
    x_sl = [sba(f"x{i}", [128, E], F32) for i in range(2)]
    xlnbf = [sba(f"xlnbf{i}", [128, E], BF16) for i in range(2)]
    xT = sba("xT", [128, 16 * 512], BF16)
    qT = sba("qT", [128, 16 * 512], BF16)
    kT_sl = [sba(f"kT{i}", [128, KV * 512], BF16) for i in range(2)]
    v_sl = [sba(f"v{i}", [128, KV * 4 * 128], BF16) for i in range(2)]
    attS = sba("attS", [128, 4 * 640], BF16)
    attS2 = attS
    attT = attS
    xl2T = sba("xl2T", [128, 16 * 512], BF16)
    pair_ctr = [0]
    aoT = sba("aoT", [128, 16 * 512], BF16)
    xmid_sl = [sba(f"xmid{i}", [128, E], F32) for i in range(4)]
    cos_sl = [sba(f"cos{i}", [128, 512], F32) for i in range(2)]
    sin_sl = [sba(f"sin{i}", [128, 512], F32) for i in range(2)]
    ropeA = sba("ropeA", [128, 512], F32)
    ropeB = sba("ropeB", [128, 512], F32)

    stat6 = [sba(f"stat6_{i}", [128, 24], F32) for i in range(4)]
    mv = [sba(f"mv{i}", [128, 2], F32) for i in range(4)]
    istd = [sba(f"istd{i}", [128, 1], F32) for i in range(4)]
    nmi = [sba(f"nmi{i}", [128, 1], F32) for i in range(4)]
    lvar = [sba(f"lvar{i}", [128, 1], F32) for i in range(4)]
    nm4 = sba("nm4", [128, 4], F32)
    nms = sba("nms", [128, 4], F32)
    den = sba("den", [128, 4], F32)
    rden = sba("rden", [128, 4], F32)
    rl = [sba(f"rl{i}", [128, NEXP], F32) for i in range(4)]
    rz = [sba(f"rz{i}", [128, NEXP], F32) for i in range(4)]
    rind = [sba(f"rind{i}", [128, NEXP], F32) for i in range(4)]
    rm1 = [sba(f"rm1_{i}", [128, 1], F32) for i in range(4)]
    rm2 = [sba(f"rm2_{i}", [128, 1], F32) for i in range(4)]
    rtmp = [sba(f"rtmp{i}", [128, NEXP], F32) for i in range(4)]
    rden1 = [sba(f"rden1_{i}", [128, 1], F32) for i in range(4)]
    rrec = [sba(f"rrec{i}", [128, 1], F32) for i in range(4)]
    if DEBUG:
        print("SBUF remaining after phase-A allocs:", nc.sbuf_bytes_remaining)

    ps = contextlib.ExitStack()
    PS_p = [ps.enter_context(nc.psum_tensor(f"PS_p{i}", [128, 1536], F32))
            for i in range(2)]
    PS_t = ps.enter_context(nc.psum_tensor("PS_t", [128, 2048], BF16))

    P = Prog(nc)
    b_consts = Buf("consts")
    b_wrt = Buf("wrt")
    b_wr = Buf("w_r")
    b_wq = [Buf(f"wq{i}") for i in range(2)]
    b_wos = [Buf(f"wos{i}") for i in range(2)]
    b_x = [Buf(f"x{i}") for i in range(2)]
    b_xlnbf = [Buf(f"xlnbf{i}") for i in range(2)]
    b_xT = Buf("xT")
    b_qT = Buf("qT")
    b_kT = [Buf(f"kT{i}") for i in range(2)]
    b_v = [Buf(f"v{i}") for i in range(2)]
    b_attSh = [[Buf(f"attS{s}{h}") for h in range(2)] for s in range(2)]
    b_xl2T = Buf("xl2T")
    b_bq = Buf("bq")
    b_k2 = Buf("k2")
    b_aoT = Buf("aoT")
    b_xmid = [Buf(f"xmid{i}") for i in range(4)]
    b_cos = [Buf(f"cos{i}") for i in range(2)]
    b_ropeA = Buf("ropeA")
    b_ropeB = Buf("ropeB")

    b_stat = [Buf(f"stat{i}") for i in range(4)]
    b_lnsc = [Buf(f"lnsc{i}") for i in range(4)]
    b_attst = [Buf("attst0"), Buf("attst1")]
    b_rt = [Buf(f"router{i}") for i in range(4)]
    b_PSp = [Buf("PS_p0"), Buf("PS_p1")]
    b_PS_t = [Buf("PS_t0"), Buf("PS_t1")]
    b_xmid_d = Buf("xmid_d")
    b_xl2t_d = Buf("xl2t_d")

    P.step("sync", lambda E: E.dma_start(ident[:], ident_in), writes=[b_consts], dma="Ds")
    P.step("sync", lambda E: E.dma_start(maskw[:], maskw_in), writes=[b_consts], dma="Ds")
    P.step("sync", lambda E: E.dma_start(maskc[:], maskc_in), writes=[b_consts], dma="Ds")
    P.step("sync", lambda E: E.dma_start(hbias[:], hbias_in), writes=[b_consts], dma="Ds")
    P.step("sync", lambda E: E.dma_start(maskwh[:], maskwh_in), writes=[b_consts], dma="Ds")
    P.step("sync", lambda E: E.dma_start(maskwb[:], maskwb_in), writes=[b_consts], dma="Ds")
    P.step("sync", lambda E: E.dma_start(maskcb[:], maskcb_in), writes=[b_consts], dma="Ds")
    P.step("sync", lambda E: E.dma_start(maskwhb[:], maskwhb_in), writes=[b_consts], dma="Ds")
    P.step("sync", lambda E: E.dma_start(hbiasb[:], hbiasb_in), writes=[b_consts], dma="Ds")
    P.step("sync", lambda E: E.dma_start(onesc[:], onesc_in), writes=[b_consts], dma="Ds")
    P.step("sync", lambda E: E.dma_start(onesr[:], onesr_in), writes=[b_consts], dma="Ds")
    P.step("gpsimd", lambda E: E.dma_start(
        wrt_s.rearrange("p (c f) -> p c f", c=16),
        wrt.rearrange("(c p) f -> p c f", p=128)), writes=[b_wrt], dma="Dg")
    P.step("vector", lambda E: E.memset(epsc[:], 1e-5), writes=[b_consts])
    P.step("vector", lambda E: E.memset(K2f[:], 0.0), writes=[b_k2])
    P.step("vector", lambda E: E.memset(K2mb[:], 0.0), writes=[b_k2])

    def ln_bf(tiles):
        for k, xbuf, bx, ob, bo in tiles:
            def bn(E, k=k, xbuf=xbuf):
                for i in range(4):
                    ins = E.bn_stats(stat6[k][:, i * 6:(i + 1) * 6],
                                     xbuf[:, i * 512:(i + 1) * 512])
                return ins
            P.step("vector", bn, reads=[bx], writes=[b_stat[k]])
        for k, xbuf, bx, ob, bo in tiles:
            P.step("vector", lambda E, k=k: E.bn_aggr(mv[k][:], stat6[k][:]),
                   reads=[b_stat[k]], writes=[b_lnsc[k]])
        for k, xbuf, bx, ob, bo in tiles:
            P.step("scalar", lambda E, k=k: E.activation(
                lvar[k][:], mv[k][:, 1:2], AF.Ln, bias=epsc[:]),
                reads=[b_lnsc[k], b_consts], writes=[b_lnsc[k]])
        for k, xbuf, bx, ob, bo in tiles:
            P.step("scalar", lambda E, k=k: E.activation(
                istd[k][:], lvar[k][:], AF.Exp, scale=-0.5),
                reads=[b_lnsc[k]], writes=[b_lnsc[k]])
        for k, xbuf, bx, ob, bo in tiles:
            P.step("vector", lambda E, k=k: E.tensor_scalar(
                nmi[k][:], mv[k][:, 0:1], -1.0, istd[k][:], OP.mult, OP.mult),
                reads=[b_lnsc[k]], writes=[b_lnsc[k]])
        for k, xbuf, bx, ob, bo in tiles:
            P.step("scalar", lambda E, k=k, xbuf=xbuf, ob=ob: E.activation(
                ob[:], xbuf[:], AF.Identity, bias=nmi[k][:], scale=istd[k][:]),
                reads=[bx, b_lnsc[k]], writes=[bo])

    def transpose16(srcbuf, b_src, dstbuf, b_dst, dst_col):
        for g in range(4):
            sl = g % 2
            off = sl * 1024

            def tr(E, g=g, off=off):
                for k in range(4):
                    mc = g * 4 + k
                    ins = E.transpose(PS_t[:, off + k * 128: off + (k + 1) * 128],
                                      srcbuf[:, mc * 128:(mc + 1) * 128], ident[:])
                return ins
            P.step("tensor", tr, reads=[b_src, b_consts], writes=[b_PS_t[sl]])
            dv = dstbuf.rearrange("p (c t) -> p c t", c=16)[:, g * 4:(g + 1) * 4,
                                                           dst_col:dst_col + 128]
            P.step("vector", lambda E, dv=dv, off=off: E.tensor_copy(
                dv, PS_t[:, off:off + 512].rearrange("p (c t) -> p c t", c=4)),
                reads=[b_PS_t[sl]], writes=[b_dst])

    def emit_ln1_qkv(s):
        owned = s >= 1
        sbp = s % 2
        P.step("sync", lambda E, s=s: E.dma_start(
            cos_sl[s % 2][:], cost[:, s * 512:(s + 1) * 512]),
            writes=[b_cos[s % 2]], dma="Ds")
        P.step("sync", lambda E, s=s: E.dma_start(
            sin_sl[s % 2][:], sint[:, s * 512:(s + 1) * 512]),
            writes=[b_cos[s % 2]], dma="Ds")

        for tp2 in range(2):
            tiles = []
            for xs in range(2):
                tt = tp2 * 2 + xs
                row0 = s * 512 + tt * 128
                P.step("sync", lambda E, row0=row0, xs=xs: E.dma_start(
                    x_sl[xs][:], x_in[row0:row0 + 128, :]), writes=[b_x[xs]], dma="Ds")
                tiles.append((xs, x_sl[xs], b_x[xs], xlnbf[xs], b_xlnbf[xs]))
            ln_bf(tiles)
            for xs in range(2):
                tt = tp2 * 2 + xs
                transpose16(xlnbf[xs], b_xlnbf[xs], xT, b_xT, tt * 128)

        ft_list = (list(range(16, 24)) + list(range(16))) if owned else list(range(16, 24))
        groups = [ft_list[i:i + 2] for i in range(0, len(ft_list), 2)]
        for gi, grp in enumerate(groups):
            wslot, bws = wq_sl[gi % 2], b_wq[gi % 2]
            f0 = grp[0] * 128
            nf = len(grp) * 128

            def wdma(E, f0=f0, nf=nf, wslot=wslot):
                return [E.dma_start(
                    wslot[:, u * 4 * nf:(u + 1) * 4 * nf].rearrange(
                        "p (c f) -> p c f", c=4),
                    wqkv[u * 512:(u + 1) * 512, f0:f0 + nf].rearrange(
                        "(c p) f -> p c f", p=128)) for u in range(4)]
            P.step("gpsimd", wdma, writes=[bws], dma="Dg", ndma=4)
            for fi, ft in enumerate(grp):
                psf = PS_p[ft % 2]
                psq, bpsq = psf[:, 0:512], b_PSp[ft % 2]

                def mm(E, psq=psq, wslot=wslot, fi=fi, nf=nf):
                    for mc in range(16):
                        ins = E.matmul(psq[:, 0:512],
                                       wslot[:, mc * nf + fi * 128: mc * nf + (fi + 1) * 128],
                                       xT[:, mc * 512:(mc + 1) * 512],
                                       start=(mc == 0), stop=(mc == 15))
                    return ins
                P.step("tensor", mm, reads=[bws, b_xT], writes=[bpsq])
                if ft < 20:
                    sqc = xlnbf[1][:, 0:512]
                    if ft < 16:
                        P.step("scalar", lambda E, psq=psq, sqc=sqc: E.activation(
                            sqc, psq[:, 0:512], AF.Square),
                            reads=[bpsq], writes=[b_xlnbf[1]])
                        kv2 = ft // 4

                        def qn2(E, psf=psf, sqc=sqc, kv2=kv2):
                            for qt in range(4):
                                E.matmul(psf[:, 512 + qt:513 + qt],
                                         sqc[:, qt * 128:(qt + 1) * 128],
                                         onesc[:], start=True, stop=False)
                                ins = E.matmul(psf[:, 512 + qt:513 + qt],
                                               onesr[:], K2mb[:, kv2:kv2 + 1],
                                               start=False, stop=True)
                            return ins
                        P.step("tensor", qn2, reads=[b_xlnbf[1], b_consts, b_k2],
                               writes=[bpsq])
                        P.step("vector", lambda E, psf=psf, ft=ft: E.tensor_scalar(
                            bq[:, ft * 4:(ft + 1) * 4], psf[:, 512:516],
                            -0.5 * SCALE, -2.0, OP.mult, OP.add),
                            reads=[bpsq], writes=[b_bq])
                        dst, bdst = qT[:, ft * 512:(ft + 1) * 512], b_qT
                    else:
                        P.step("scalar", lambda E, psq=psq, sqc=sqc: E.activation(
                            sqc, psq[:, 0:512], AF.Square),
                            reads=[bpsq], writes=[b_xlnbf[1]])
                        P.step("tensor", lambda E, psf=psf, sqc=sqc: E.matmul(
                            psf[0:1, 512:1024], onesc[:], sqc, start=True, stop=True),
                            reads=[b_xlnbf[1], b_consts], writes=[bpsq])
                        kv2 = ft - 16
                        P.step("vector", lambda E, psf=psf: E.reduce_max(
                            k2t[:], psf[0:1, 512:1024], axis=AX),
                            reads=[bpsq], writes=[b_k2])
                        P.step("vector", lambda E, kv2=kv2: E.tensor_max(
                            K2f[:, kv2:kv2 + 1], K2f[:, kv2:kv2 + 1], k2t[:]),
                            reads=[b_k2], writes=[b_k2])
                        P.step("vector", lambda E, kv2=kv2: E.tensor_copy(
                            K2mb[:, kv2:kv2 + 1], K2f[:, kv2:kv2 + 1]),
                            reads=[b_k2], writes=[b_k2])
                        dst = kT_sl[sbp][:, (ft - 16) * 512:(ft - 15) * 512]
                        bdst = b_kT[sbp]
                    cs, sn, bc = cos_sl[s % 2], sin_sl[s % 2], b_cos[s % 2]
                    P.step("vector", lambda E, psq=psq, cs=cs: E.tensor_mul(
                        ropeA[:], psq[:, 0:512], cs[:]),
                        reads=[bpsq, bc], writes=[b_ropeA])

                    def bmul(E, psq=psq, sn=sn):
                        E.tensor_mul(ropeB[0:64, :], psq[64:128, 0:512], sn[0:64, :])
                        return E.tensor_mul(ropeB[64:128, :], psq[0:64, 0:512],
                                            sn[64:128, :])
                    P.step("vector", bmul, reads=[bpsq, bc], writes=[b_ropeB])
                    P.step("vector", lambda E, dst=dst: E.tensor_add(
                        dst, ropeA[:], ropeB[:]),
                        reads=[b_ropeA, b_ropeB], writes=[bdst])
                else:
                    kv = ft - 20
                    vtmp = xlnbf[0][:, 0:512]
                    P.step("vector", lambda E, psq=psq, vtmp=vtmp: E.tensor_copy(
                        vtmp, psq[:, 0:512]), reads=[bpsq], writes=[b_xlnbf[0]])

                    def vtr(E, vtmp=vtmp):
                        for u in range(4):
                            ins = E.transpose(PS_t[:, u * 128:(u + 1) * 128],
                                              vtmp[:, u * 128:(u + 1) * 128], ident[:])
                        return ins
                    P.step("tensor", vtr, reads=[b_xlnbf[0], b_consts], writes=[b_PS_t[0]])
                    P.step("vector", lambda E, kv=kv, sbp=sbp: E.tensor_copy(
                        v_sl[sbp][:, kv * 512:(kv + 1) * 512], PS_t[:, 0:512]),
                        reads=[b_PS_t[0]], writes=[b_v[sbp]])


    def emit_attention(s):
        def att_scores(pi, QT, qtl, kv, hp):
            sl = pi % 2
            PS = PS_p[sl]
            bPS = b_PSp[sl]

            def scores(E, QT=QT, qtl=qtl, kv=kv, hp=hp, PS=PS):
                masked = {0, 4} | {j for j in range(1, 4) if QT + j <= 3}
                for hhp in range(2):
                    h = kv * 4 + 2 * hp + hhp
                    qsl = qT[:, h * 512 + qtl * 128: h * 512 + (qtl + 1) * 128]
                    for j in range(5):
                        J = QT + j
                        ksl = kT_sl[(J // 4) % 2][:, kv * 512 + (J % 4) * 128:
                                                  kv * 512 + (J % 4 + 1) * 128]
                        out = PS[:, hhp * 640 + j * 128: hhp * 640 + (j + 1) * 128]
                        ins = E.matmul(out, qsl, ksl,
                                       start=True, stop=(j not in masked))
                        if j in masked:
                            if j == 0:
                                m = maskwhb if QT <= 3 else maskwb
                            elif j == 4:
                                m = maskcb
                            else:
                                m = hbiasb
                            ins = E.matmul(out, ident[:], m[:],
                                           start=False, stop=True)
                return ins
            P.step("tensor", scores, reads=[b_qT, b_kT[0], b_kT[1], b_consts],
                   writes=[bPS])

        def att_frontA(pi, QT, qtl, kv, hp):
            sl = pi % 2
            PS = PS_p[sl]
            bPS = b_PSp[sl]
            aoff = sl * 1280
            coff = sl * 2
            bst = b_attst[sl]
            bSh = b_attSh[sl]

            def exps(E, PS=PS, aoff=aoff, coff=coff, QT=QT, qtl=qtl, kv=kv, hp=hp):
                for hhp in range(2):
                    h = kv * 4 + 2 * hp + hhp
                    ins = E.activation(
                        attS[:, aoff + hhp * 640: aoff + (hhp + 1) * 640],
                        PS[:, hhp * 640:(hhp + 1) * 640], AF.Exp,
                        bias=bq[:, h * 4 + qtl: h * 4 + qtl + 1], scale=SCALE,
                        accum_out=den[:, coff + hhp:coff + hhp + 1])
                return ins
            P.step("scalar", exps, reads=[bPS, b_bq, bst],
                   writes=[bSh[0], bSh[1], bst])

        def att_frontC(pi, QT, qtl, kv, hp):
            sl = pi % 2
            aoff = sl * 1280
            coff = sl * 2
            bst = b_attst[sl]
            bSh = b_attSh[sl]
            P.step("vector", lambda E, coff=coff: E.reciprocal(
                rden[:, coff:coff + 2], den[:, coff:coff + 2]),
                reads=[bst], writes=[bst])

            def attscale(E, aoff=aoff, coff=coff):
                for hhp in range(2):
                    ins = E.tensor_scalar_mul(
                        attS2[:, aoff + hhp * 640: aoff + (hhp + 1) * 640],
                        attS[:, aoff + hhp * 640: aoff + (hhp + 1) * 640],
                        rden[:, coff + hhp:coff + hhp + 1])
                return ins
            P.step("vector", attscale, reads=[bst],
                   writes=[bSh[0], bSh[1]])

        def att_back(pi, QT, qtl, kv, hp):
            sl = pi % 2
            PS = PS_p[sl]
            bPS = b_PSp[sl]
            aoff = sl * 1280
            bSh = b_attSh[sl]
            for hhp in range(2):
                ti = hhp
                toff = ti * 1024

                def tr(E, aoff=aoff, hhp=hhp, toff=toff):
                    for j in range(5):
                        ins = E.transpose(
                            PS_t[:, toff + j * 128: toff + (j + 1) * 128],
                            attS2[:, aoff + hhp * 640 + j * 128:
                                  aoff + hhp * 640 + (j + 1) * 128],
                            ident[:])
                    return ins
                P.step("tensor", tr, reads=[bSh[hhp], b_consts], writes=[b_PS_t[ti]])
                ev_eng = "vector" if hhp == 0 else "scalar"

                def evacT(E, aoff=aoff, hhp=hhp, toff=toff):
                    if hhp == 0:
                        return E.tensor_copy(
                            attT[:, aoff + hhp * 640: aoff + (hhp + 1) * 640],
                            PS_t[:, toff:toff + 640])
                    return E.copy(
                        attT[:, aoff + hhp * 640: aoff + (hhp + 1) * 640],
                        PS_t[:, toff:toff + 640])
                P.step(ev_eng, evacT, reads=[b_PS_t[ti]], writes=[bSh[hhp]])
            for hhp in range(2):
                def av(E, QT=QT, kv=kv, hhp=hhp, aoff=aoff, PS=PS):
                    for j in range(5):
                        J = QT + j
                        vsl = v_sl[(J // 4) % 2][:, (kv * 4 + J % 4) * 128:
                                                 (kv * 4 + J % 4 + 1) * 128]
                        ins = E.matmul(
                            PS[:, 1280 + hhp * 128:1280 + (hhp + 1) * 128],
                            vsl,
                            attT[:, aoff + hhp * 640 + j * 128:
                                 aoff + hhp * 640 + (j + 1) * 128],
                            start=(j == 0), stop=(j == 4))
                    return ins
                P.step("tensor", av, reads=[bSh[hhp], b_v[0], b_v[1]], writes=[bPS])

            h0 = kv * 4 + 2 * hp
            dv = aoT.rearrange("p (h t) -> p h t", h=16)[:, h0:h0 + 2,
                                                         qtl * 128:(qtl + 1) * 128]
            P.step("vector", lambda E, dv=dv, PS=PS: E.tensor_copy(
                dv, PS[:, 1280:1536].rearrange("p (h t) -> p h t", h=2)),
                reads=[bPS], writes=[b_aoT])

        pairs = [((s - 1) * 4 + qtl, qtl, kv, hp)
                 for qtl in range(4) for kv in range(KV) for hp in range(2)]
        prev = None
        for args in pairs:
            pi = pair_ctr[0]
            pair_ctr[0] += 1
            att_scores(pi, *args)
            att_frontA(pi, *args)
            if prev is not None:
                att_back(*prev)
            att_frontC(pi, *args)
            prev = (pi, *args)
        att_back(*prev)

    def emit_wo_ln2(s):
        for mq in range(4):
            wsl, bwsl = wo_sl[mq % 2], b_wos[mq % 2]

            def wodma(E, mq=mq, wsl=wsl):
                return [E.dma_start(
                    wsl[:, u * 4 * 512:(u + 1) * 4 * 512].rearrange(
                        "p (c f) -> p c f", c=4),
                    wo_in[u * 512:(u + 1) * 512,
                          mq * 512:(mq + 1) * 512].rearrange(
                        "(c p) f -> p c f", p=128)) for u in range(4)]
            P.step("gpsimd", wodma, writes=[bwsl], dma="Dg", ndma=4)

            def womm(E, wsl=wsl):
                for tt in range(4):
                    PS = PS_p[tt // 2]
                    for fc in range(16):
                        ins = E.matmul(PS[:, (tt % 2) * 512:(tt % 2 + 1) * 512],
                                       aoT[:, fc * 512 + tt * 128:
                                           fc * 512 + (tt + 1) * 128],
                                       wsl[:, fc * 512:(fc + 1) * 512],
                                       start=(fc == 0), stop=(fc == 15))
                return ins
            P.step("tensor", womm, reads=[b_aoT, bwsl],
                   writes=[b_PSp[0], b_PSp[1]])
            for tt in range(4):
                eng = "vector" if tt % 2 == 0 else "scalar"

                def evac(E, mq=mq, tt=tt, eng=eng):
                    PSsl = PS_p[tt // 2][:, (tt % 2) * 512:(tt % 2 + 1) * 512]
                    dst = xmid_sl[tt][:, mq * 512:(mq + 1) * 512]
                    if eng == "vector":
                        return E.tensor_copy(dst, PSsl)
                    return E.copy(dst, PSsl)
                P.step(eng, evac, reads=[b_PSp[tt // 2]], writes=[b_xmid[tt]])
        for tt in range(4):
            row0 = s * 512 + tt * 128
            xs = tt % 2
            xm, bxm = xmid_sl[tt], b_xmid[tt]
            if tt % 2 == 0:
                for xs2 in range(2):
                    P.step("sync", lambda E, row0=s * 512 + (tt + xs2) * 128, xs2=xs2:
                           E.dma_start(x_sl[xs2][:], x_in[row0:row0 + 128, :]),
                           writes=[b_x[xs2]], dma="Ds")
            P.step("vector", lambda E, xm=xm, xs=xs: E.tensor_add(
                xm[:], xm[:], x_sl[xs][:]),
                reads=[b_x[xs]], writes=[bxm])
        for tt in range(4):
            t_owned = (s - 1) * 4 + tt
            xm, bxm = xmid_sl[tt], b_xmid[tt]
            P.step("sync", lambda E, t_owned=t_owned, xm=xm: E.dma_start(
                xmid_d[t_owned * 128:(t_owned + 1) * 128, :], xm[:]),
                reads=[bxm], writes=[b_xmid_d], dma="Ds")

        ln_bf([(tt, xmid_sl[tt], b_xmid[tt], xlnbf[tt % 2], b_xlnbf[tt % 2])
               for tt in (0, 1)])
        for tt in (0, 1):
            transpose16(xlnbf[tt % 2], b_xlnbf[tt % 2], xl2T, b_xl2T, tt * 128)
        ln_bf([(tt, xmid_sl[tt], b_xmid[tt], xlnbf[tt % 2], b_xlnbf[tt % 2])
               for tt in (2, 3)])
        for tt in (2, 3):
            transpose16(xlnbf[tt % 2], b_xlnbf[tt % 2], xl2T, b_xl2T, tt * 128)

    def emit_router_spill(s):
        for tt in range(4):
            def rmm(E, tt=tt):
                for mc in range(16):
                    ins = E.matmul(PS_p[1][:, tt * 32:tt * 32 + NEXP],
                                   xl2T[:, mc * 512 + tt * 128:
                                        mc * 512 + (tt + 1) * 128],
                                   wrt_s[:, mc * NEXP:(mc + 1) * NEXP],
                                   start=(mc == 0), stop=(mc == 15))
                return ins
            P.step("tensor", rmm, reads=[b_xl2T, b_wrt], writes=[b_PSp[1]])
            P.step("vector", lambda E, tt=tt: E.tensor_copy(
                rl[tt][:], PS_p[1][:, tt * 32:tt * 32 + NEXP]),
                reads=[b_PSp[1]], writes=[b_rt[tt]])
        for tt in range(4):
            P.step("vector", lambda E, tt=tt: E.reduce_max(
                rm1[tt][:], rl[tt][:], axis=AX, negate=True),
                reads=[b_rt[tt]], writes=[b_rt[tt]])
        for tt in range(4):
            P.step("vector", lambda E, tt=tt: E.tensor_scalar(
                rz[tt][:], rl[tt][:], -1.0, rm1[tt][:], OP.mult, OP.is_le),
                reads=[b_rt[tt]], writes=[b_rt[tt]])
        for tt in range(4):
            P.step("vector", lambda E, tt=tt: E.scalar_tensor_tensor(
                rtmp[tt][:], rz[tt][:], -1e30, rl[tt][:], OP.mult, OP.add),
                reads=[b_rt[tt]], writes=[b_rt[tt]])
        for tt in range(4):
            P.step("vector", lambda E, tt=tt: E.reduce_max(
                rm2[tt][:], rtmp[tt][:], axis=AX, negate=True),
                reads=[b_rt[tt]], writes=[b_rt[tt]])
        for tt in range(4):
            P.step("vector", lambda E, tt=tt: E.tensor_scalar(
                rind[tt][:], rl[tt][:], -1.0, rm2[tt][:], OP.mult, OP.is_le),
                reads=[b_rt[tt]], writes=[b_rt[tt]])
        for tt in range(4):
            P.step("scalar", lambda E, tt=tt: E.activation(
                rz[tt][:], rl[tt][:], AF.Exp, bias=rm1[tt][:]),
                reads=[b_rt[tt]], writes=[b_rt[tt]])
        for tt in range(4):
            P.step("vector", lambda E, tt=tt: E.scalar_tensor_tensor(
                rtmp[tt][:], rz[tt][:], 1.0, rind[tt][:], OP.mult, OP.mult,
                accum_out=rden1[tt][:]),
                reads=[b_rt[tt]], writes=[b_rt[tt]])
        for tt in range(4):
            P.step("vector", lambda E, tt=tt: E.reciprocal(
                rrec[tt][:], rden1[tt][:]),
                reads=[b_rt[tt]], writes=[b_rt[tt]])
        for tt in range(4):
            t_owned = (s - 1) * 4 + tt
            P.step("vector", lambda E, tt=tt, t_owned=t_owned: E.tensor_scalar_mul(
                w_r[:, t_owned * NEXP:(t_owned + 1) * NEXP], rtmp[tt][:], rrec[tt][:]),
                reads=[b_rt[tt]], writes=[b_wr])


        def xl2spill(E, s=s):
            return [E.dma_start(
                xl2t_d[g * 512:(g + 1) * 512, (s - 1) * 512:s * 512].rearrange(
                    "(c p) t -> p c t", p=128),
                xl2T[:, g * 4 * 512:(g + 1) * 4 * 512].rearrange(
                    "p (c t) -> p c t", c=4)) for g in range(4)]
        P.step("sync", xl2spill, reads=[b_xl2T], writes=[b_xl2t_d], dma="Ds", ndma=4)


    emit_ln1_qkv(0)
    emit_ln1_qkv(1)
    for s in range(1, 5):
        emit_attention(s)
        emit_wo_ln2(s)
        if s < 4:
            emit_ln1_qkv(s + 1)
        emit_router_spill(s)

    P.barrier()
    ps.close()
    esA.close()
    PGU = es.enter_context(nc.psum_tensor("PGU", [128, 2048], F32))
    PD = es.enter_context(nc.psum_tensor("PD", [128, 2048], F32))
    b_PG, b_PU = Buf("PG"), Buf("PU")
    b_PD = [Buf("PDa"), Buf("PDb")]

    acc = sb("acc", [128, 8 * 2048], F32)
    xl2h = sb("xl2h", [128, 16 * 1024], BF16)
    wsl2 = [sb(f"moew{i}", [128, 16384], BF16) for i in range(2)]
    hT = sb("hT", [128, 8 * 1024], BF16)
    sg = sb("sg", [128, 1024], F32)
    b_acc_t = [Buf(f"acc{i}") for i in range(8)]
    b_xl2h = Buf("xl2h")
    b_w2 = [Buf(f"moew{i}") for i in range(2)]
    b_hT = Buf("hT")
    b_sg = Buf("sg")
    if DEBUG:
        print("SBUF remaining after MoE allocs:", nc.sbuf_bytes_remaining)

    st_tok = []

    def emit_accload(half):
        for u in range(2):
            def accload(E, half=half, u=u):
                return [E.dma_start(
                    acc[:, u * 4 * 2048:(u + 1) * 4 * 2048].rearrange(
                        "p (c m) -> p c m", c=4),
                    xmid_d[half * 1024 + u * 512: half * 1024 + (u + 1) * 512,
                           :].rearrange("(c p) m -> p c m", p=128))]
            P.step("sync", accload, reads=[b_xmid_d],
                   writes=[b_acc_t[4 * u + i] for i in range(4)], dma="Ds", ndma=1)

    def emit_xl2load(half):
        def xl2load(E, half=half):
            return [E.dma_start(
                xl2h[:, g * 4 * 1024:(g + 1) * 4 * 1024].rearrange(
                    "p (c t) -> p c t", c=4),
                xl2t_d[g * 512:(g + 1) * 512,
                       half * 1024:(half + 1) * 1024].rearrange(
                    "(c p) t -> p c t", p=128)) for g in range(4)]
        P.step("sync", xl2load, reads=[b_xl2t_d], writes=[b_xl2h], dma="Ds", ndma=4)

    def emit_gu_dma(e, fh, sl):
        wsl, bw = wsl2[sl], b_w2[sl]

        def both(E, e=e, fh=fh, wsl=wsl):
            return E.dma_start(
                wsl[:, :].rearrange("p (g c f) -> p g c f", g=4, c=4),
                wgu_in[e, fh])
        P.step("gpsimd", both, writes=[bw], dma="Dg")
        return sl

    def emit_wd_dma(e, sl):
        wsl, bw = wsl2[sl], b_w2[sl]

        def wddma(E, e=e, wsl=wsl):
            return E.dma_start(
                wsl[:, :].rearrange("p (c m) -> p c m", c=8),
                wd_in[e])
        P.step("gpsimd", wddma, writes=[bw], dma="Dg")
        return sl

    if True:
        def emit_gu_chunk(e, fh, fc, sl):
            hc = fh * 4 + fc
            wsl, bw = wsl2[sl], b_w2[sl]

            def gummG(E, wsl=wsl, fc=fc):
                for mc in range(16):
                    for h2 in range(2):
                        ins = E.matmul(
                            PGU[:, h2 * 512:(h2 + 1) * 512],
                            wsl[:, mc * 1024 + fc * 128: mc * 1024 + (fc + 1) * 128],
                            xl2h[:, mc * 1024 + h2 * 512: mc * 1024 + (h2 + 1) * 512],
                            start=(mc == 0), stop=(mc == 15))
                return ins
            P.step("tensor", gummG, reads=[bw, b_xl2h], writes=[b_PG])

            def gummU(E, wsl=wsl, fc=fc):
                for mc in range(16):
                    for h2 in range(2):
                        ins = E.matmul(
                            PGU[:, 1024 + h2 * 512:1024 + (h2 + 1) * 512],
                            wsl[:, mc * 1024 + 512 + fc * 128:
                                mc * 1024 + 512 + (fc + 1) * 128],
                            xl2h[:, mc * 1024 + h2 * 512: mc * 1024 + (h2 + 1) * 512],
                            start=(mc == 0), stop=(mc == 15))
                return ins
            P.step("tensor", gummU, reads=[bw, b_xl2h], writes=[b_PU])
            P.step("scalar", lambda E: E.activation(sg[:], PGU[:, 0:1024], AF.Silu),
                   reads=[b_PG], writes=[b_sg])
            P.step("vector", lambda E, hc=hc: E.tensor_mul(
                hT[:, hc * 1024:(hc + 1) * 1024], sg[:], PGU[:, 1024:2048]),
                reads=[b_sg, b_PU], writes=[b_hT])

        def emit_down(half, e, sl):
            wsl, bw = wsl2[sl], b_w2[sl]
            for tt in range(8):
                t_owned = half * 8 + tt
                for mh in range(2):
                    def dmm(E, wsl=wsl, tt=tt, mh=mh):
                        for hc in range(8):
                            for m in range(2):
                                ins = E.matmul(
                                    PD[:, mh * 1024 + m * 512: mh * 1024 + (m + 1) * 512],
                                    hT[:, hc * 1024 + tt * 128: hc * 1024 + (tt + 1) * 128],
                                    wsl[:, hc * 2048 + mh * 1024 + m * 512:
                                        hc * 2048 + mh * 1024 + (m + 1) * 512],
                                    start=(hc == 0), stop=(hc == 7))
                        return ins
                    P.step("tensor", dmm, reads=[bw, b_hT], writes=[b_PD[mh]])
                    P.step("vector", lambda E, e=e, tt=tt, mh=mh, t_owned=t_owned:
                           E.scalar_tensor_tensor(
                               acc[:, tt * 2048 + mh * 1024: tt * 2048 + (mh + 1) * 1024],
                               PD[:, mh * 1024:(mh + 1) * 1024],
                               w_r[:, t_owned * NEXP + e: t_owned * NEXP + e + 1],
                               acc[:, tt * 2048 + mh * 1024: tt * 2048 + (mh + 1) * 1024],
                               OP.mult, OP.add),
                           reads=[b_PD[mh], b_wr, b_acc_t[tt]], writes=[b_acc_t[tt]])
                if e == NEXP - 1:
                    def sttt(E, half=half, tt=tt):
                        return [E.dma_start(
                            out_hbm[half * 1024 + tt * 128: half * 1024 + (tt + 1) * 128, :],
                            acc[:, tt * 2048:(tt + 1) * 2048])]
                    st_tok.append(P.step("sync", sttt, reads=[b_acc_t[tt]],
                                         dma="Ds", ndma=1))

        seq = [(h2, e2) for h2 in range(2) for e2 in range(NEXP)]
        emit_accload(0)
        emit_xl2load(0)
        s_fh0 = emit_gu_dma(0, 0, 0)
        s_fh1 = emit_gu_dma(0, 1, 1)
        for idx, (half, e) in enumerate(seq):
            nxt = seq[idx + 1] if idx + 1 < len(seq) else None
            for fc in range(4):
                emit_gu_chunk(e, 0, fc, s_fh0)
            s_wd = emit_wd_dma(e, s_fh0)
            for fc in range(4):
                emit_gu_chunk(e, 1, fc, s_fh1)
            if nxt is not None:
                if nxt == (1, 0):
                    emit_xl2load(1)
                s_fh0n = emit_gu_dma(nxt[1], 0, s_fh1)
            emit_down(half, e, s_wd)
            if nxt is not None:
                if nxt == (1, 0):
                    emit_accload(1)
                s_fh1n = emit_gu_dma(nxt[1], 1, s_wd)
                s_fh0, s_fh1 = s_fh0n, s_fh1n



    for t in st_tok:
        for pair in t:
            P._emit_wait("sync", pair)

    with nc.Block() as block:
        P.finalize(block)
    es.close()
    return nc


_NC_CACHE = []


def _rope_tables(pos):
    inv = (1.0 / (THETA ** (np.arange(0, D, 2, dtype=np.float32) / D))).astype(np.float32)
    ang = pos[None, :].astype(np.float32) * inv[:, None]
    cos = np.cos(ang).astype(np.float32)
    sin = np.sin(ang).astype(np.float32)
    cost = np.concatenate([cos, cos], axis=0)
    sint = np.concatenate([-sin, sin], axis=0)
    return np.ascontiguousarray(cost), np.ascontiguousarray(sint)


def kernel(L0_attn_x, L0_attn_w_qkv, L0_attn_w_o, L0_ffn_router_gate,
           L0_ffn_expert_gate_up, L0_ffn_expert_down):
    x = np.ascontiguousarray(np.asarray(L0_attn_x, dtype=np.float32))
    wqkv = np.ascontiguousarray(np.asarray(L0_attn_w_qkv, dtype=np.float32))
    wo = np.ascontiguousarray(np.asarray(L0_attn_w_o, dtype=np.float32))
    wr = np.asarray(L0_ffn_router_gate, dtype=np.float32)
    wgu = np.asarray(L0_ffn_expert_gate_up, dtype=np.float32)
    wd = np.asarray(L0_ffn_expert_down, dtype=np.float32)
    wg_t = wgu[:, :, 0::2].reshape(NEXP, 4, 4, 128, 2, 512).transpose(
        0, 4, 3, 1, 2, 5)
    wu_t = wgu[:, :, 1::2].reshape(NEXP, 4, 4, 128, 2, 512).transpose(
        0, 4, 3, 1, 2, 5)
    wgu_t = np.ascontiguousarray(np.concatenate([wg_t, wu_t], axis=-1))
    wd_t = np.ascontiguousarray(
        wd.reshape(NEXP, 8, 128, 2048).transpose(0, 2, 1, 3))
    wrt = np.ascontiguousarray(wr.T)

    ident = np.eye(128, dtype=ml_dtypes.bfloat16)
    r = np.arange(128)
    maskw = np.where(r[None, :] > r[:, None], 0.0, NEG).astype(np.float32)
    maskc = np.where(r[None, :] <= r[:, None], 0.0, NEG).astype(np.float32)
    in_maps = []
    for c in range(8):
        b, o = c // 2, (c % 2) * TC
        if o == 0:
            halo = np.zeros((LH, E), dtype=np.float32)
            hb = np.full((128, 128), NEG, dtype=np.float32)
        else:
            halo = x[b, o - LH:o]
            hb = np.zeros((128, 128), dtype=np.float32)
        xs = np.ascontiguousarray(np.concatenate([halo, x[b, o:o + TC]], axis=0))
        pos = np.maximum(np.arange(o - LH, o + TC), 0)
        cost, sint = _rope_tables(pos)
        in_maps.append({
            "x": xs, "wqkv": wqkv, "wo": wo, "wrt": wrt, "wgu": wgu_t,
            "wd": wd_t, "cost": cost, "sint": sint, "ident": ident,
            "maskw": maskw, "maskc": maskc, "hbias": hb,
            "maskwh": (maskw + hb).astype(np.float32),
            "maskwb": maskw.astype(ml_dtypes.bfloat16),
            "maskcb": maskc.astype(ml_dtypes.bfloat16),
            "maskwhb": (maskw + hb).astype(ml_dtypes.bfloat16),
            "hbiasb": hb.astype(ml_dtypes.bfloat16),
            "onesc": np.ones((128, 1), dtype=ml_dtypes.bfloat16),
            "onesr": np.ones((1, 128), dtype=ml_dtypes.bfloat16),
        })

    if not _NC_CACHE:
        _NC_CACHE.append(build_nc())
    nc = _NC_CACHE[0]
    res = run_bass_kernel_spmd(nc, in_maps, core_ids=list(range(8)))
    out = np.empty((B, S, E), dtype=np.float32)
    for c in range(8):
        b, o = c // 2, (c % 2) * TC
        out[b, o:o + TC] = res.results[c]["out"]
    return out
